# version 10
# baseline (speedup 1.0000x reference)
import os
import numpy as np
import ml_dtypes
import concourse.bass as bass
import concourse.mybir as mybir
from concourse.bass_utils import run_bass_kernel_spmd

F32 = mybir.dt.float32
BF16 = mybir.dt.bfloat16
U8 = mybir.dt.uint8
AF = mybir.ActivationFunctionType
ALU = mybir.AluOpType
AX = mybir.AxisListType

S = int(os.environ.get("MK_S", "4096"))
NCORES = int(os.environ.get("MK_NCORES", "8"))
E, ECH = 2048, 16
H = 16
CH = 128
NCH = S // CH
NT = S // 128
NTILE = 512
NB1 = S // NTILE
NE, ICH, INTER = 32, 6, 768
G = min(1024, S)
NG = S // G
TPG = G // 128
NTG = G // 512
EPS = 1e-5
BIG = 1e30
INV_E = 1.0 / E
SLICE = min(512, S)
NSL = S // SLICE
CPS = SLICE // CH

(SUMV, SUMSQ, MEANV, MSQ, VARV, RSTD, NMR, NEGMAX, THRV, DENOM, RDEN, BMIN,
 LNV, NEGHALF) = range(14)

PE_, VE_, SE_ = "pe", "ve", "se"


def _consts():
    f = np.float64
    start = 2.0 ** (-2.0 ** (-(np.log2(H) - 3.0)))
    slopes = np.float32([start * start**i for i in range(H)]).astype(f)
    r = np.exp(-slopes)
    i = np.arange(CH, dtype=f)
    qd = (r[None, :] ** (i[:, None] + 1.0)).astype(np.float32)
    kd = (r[None, :] ** (CH - 1.0 - i[:, None])).astype(np.float32)
    diff = i[:, None] - i[None, :]
    dm = np.where(diff >= 0, r[:, None, None] ** np.where(diff >= 0, diff, 0.0)[None], 0.0)
    dmt = np.ascontiguousarray(dm.transpose(2, 0, 1).reshape(CH, H * CH).astype(np.float32))
    rC = [float(v) for v in (r ** float(CH))]
    freq = (1.0 / 1024.0) ** np.linspace(0.0, 1.0, 32, dtype=np.float32).astype(f)
    theta = np.arange(S, dtype=f)[:, None] * freq[None, :]
    csc = np.ascontiguousarray(np.cos(theta).T.astype(np.float32))
    css = np.ascontiguousarray(np.sin(theta).T.astype(np.float32))
    ident = np.eye(128, dtype=np.float32)
    return dict(qd=qd, kd=kd, dmt=dmt, csc=csc, css=css, ident=ident), rC


class Sched:
    def __init__(self):
        self.ops = []
        self.cnt = {}
        self.pending = {}

    def add(self, eng, fn, waits=(), inc=None, drain=False):
        w = list(waits)
        if eng in self.pending:
            w = self.pending.pop(eng) + w
        incs = []
        post = None
        if inc is not None:
            sem, amt = inc if isinstance(inc, tuple) else (inc, 1)
            self.cnt[sem] = self.cnt.get(sem, 0) + amt
            incs.append((sem, amt))
            post = self.cnt[sem]
        self.ops.append((eng, fn, w, incs, drain))
        return post

    def cur(self, sem):
        return self.cnt.get(sem, 0)

    def barrier(self, engines=("pe", "ve", "se", "sy", "gp")):
        snap = [(s, v) for s, v in self.cnt.items()]
        for e in engines:
            self.pending[e] = list(snap) + self.pending.get(e, [])


def build_nc():
    nc = bass.Bass("TRN2", target_bir_lowering=False)

    x_d = nc.dram_tensor("x", [S, E], F32, kind="ExternalInput").ap()
    wq_d = nc.dram_tensor("wq", [E, E], F32, kind="ExternalInput").ap()
    wk_d = nc.dram_tensor("wk", [E, E], F32, kind="ExternalInput").ap()
    wv_d = nc.dram_tensor("wv", [E, E], F32, kind="ExternalInput").ap()
    wo_d = nc.dram_tensor("wo", [E, E], F32, kind="ExternalInput").ap()
    wg_d = nc.dram_tensor("wg", [E, NE], F32, kind="ExternalInput").ap()
    wgu_d = nc.dram_tensor("wgu", [NE, E, 2 * INTER], F32, kind="ExternalInput").ap()
    wd_d = nc.dram_tensor("wd", [NE, INTER, E], F32, kind="ExternalInput").ap()
    rmsw_d = nc.dram_tensor("rmsw", [128, ECH], F32, kind="ExternalInput").ap()
    csc_d = nc.dram_tensor("csc", [32, S], F32, kind="ExternalInput").ap()
    css_d = nc.dram_tensor("css", [32, S], F32, kind="ExternalInput").ap()
    dmt_d = nc.dram_tensor("dmt", [128, H * 128], F32, kind="ExternalInput").ap()
    qd_d = nc.dram_tensor("qdc", [128, H], F32, kind="ExternalInput").ap()
    kd_d = nc.dram_tensor("kdc", [128, H], F32, kind="ExternalInput").ap()
    id_d = nc.dram_tensor("idm", [128, 128], F32, kind="ExternalInput").ap()
    y_d = nc.dram_tensor("y", [S, E], F32, kind="ExternalOutput").ap()

    xn_sc = nc.dram_tensor("xn_sc", [ECH, 128, S], BF16).ap()
    qk_sc = nc.dram_tensor("qk_sc", [2, H, 128, S], BF16).ap()
    tm_sc = nc.dram_tensor("tm_sc", [2, H, S, 128], BF16).ap()
    attn_sc = nc.dram_tensor("attn_sc", [H, S, 128], BF16).ap()
    xat_sc = nc.dram_tensor("xat_sc", [S, E], F32).ap()

    _, rC = _consts()

    def sb(name, shape, dt):
        return nc.alloc_sbuf_tensor(name, shape, dt).ap()

    ident = sb("ident", [128, 128], BF16)
    combine = sb("combine", [128, NT * NE], F32)
    xbuf = [sb(f"xbuf{i}", [128, E], F32) for i in range(2)]
    vecs2 = [sb(f"vecs{i}", [128, 16], F32) for i in range(2)]
    bnst2 = [sb(f"bnst{i}", [128, 24], F32) for i in range(2)]
    gt32_2 = [sb(f"gt32_{i}", [128, 64], F32) for i in range(2)]
    gt8_2 = [sb(f"gt8_{i}", [128, 48], F32) for i in range(2)]
    wg_sb = sb("wg_sb", [128, ECH * NE], BF16)
    wgu_sl = [sb(f"wgu_sl{i}", [128, ECH * 256], BF16) for i in range(2)]
    wd_sl = [sb(f"wdsl{i}", [128, ICH * 512], BF16) for i in range(2)]
    kv_sb = [sb(f"kv{i}", [128, 128], F32) for i in range(2)]
    kv_bf = [sb(f"kvbf{i}", [128, 128], BF16) for i in range(2)]
    qd_sb = sb("qdsb", [128, H], F32)
    kd_sb = sb("kdsb", [128, H], F32)
    rmsw_sb = sb("rmsw_sb", [128, ECH], F32)
    trash = sb("trash", [128, E], BF16)
    dmt_sb = sb("dmt_sb", [128, H * 128], F32)
    xn_tm = sb("xn_tm", [128, E], BF16)
    xnf = [sb(f"xnf{i}", [128, E], BF16) for i in range(2)]
    sc_bf = [sb(f"scbf{i}", [128, 128], BF16) for i in range(2)]
    tmp_int = [sb(f"tmpint{i}", [128, 128], F32) for i in range(2)]
    attn_ch = [[sb(f"attnch{i}_{j}", [128, 4 * 128], BF16) for j in range(2)]
               for i in range(2)]

    AR_BYTES = 126 * 1024
    arena = nc.alloc_sbuf_tensor("arena", [128, AR_BYTES], U8).ap()

    class Carver:
        def __init__(self):
            self.off = 0

        def take(self, nelem, dt):
            sz = {F32: 4, BF16: 2}[dt]
            self.off = (self.off + 63) & ~63
            ap = arena[:, self.off:self.off + nelem * sz].bitcast(dt)
            self.off += nelem * sz
            assert self.off <= AR_BYTES, f"arena overflow {self.off}"
            return ap

    cb = Carver()
    xnt = [cb.take(ECH * NTILE, BF16) for _ in range(3)]
    wsl = {n: [cb.take(ECH * 128, BF16) for _ in range(2)] for n in "qkv"}
    qrot_t = [cb.take(NTILE, BF16) for _ in range(2)]
    krot_t = [cb.take(NTILE, BF16) for _ in range(2)]
    vfm_t = [cb.take(NTILE, BF16) for _ in range(2)]
    ktm_t = [cb.take(NTILE, BF16) for _ in range(2)]
    vtm_t = [cb.take(NTILE, BF16) for _ in range(2)]
    rt = [cb.take(NTILE, BF16) for _ in range(4)]
    cs_c = cb.take(S, BF16)
    cs_s = cb.take(S, BF16)
    b2buf = [[[cb.take(SLICE, BF16) for _ in range(3)] for _ in range(4)]
             for _ in range(2)]
    b1b2_top = cb.off
    cc_ = Carver()
    wo_sb = cc_.take(ECH * E, BF16)
    attn_in = [cc_.take(E, BF16) for _ in range(2)]
    attnn_tm = cc_.take(E, BF16)
    attnnf = [cc_.take(ECH * 128, BF16) for _ in range(2)]
    cd = Carver()
    xn_g = cd.take(ECH * G, BF16)
    acc = cd.take(TPG * E, F32)
    h_sb = [cd.take(ICH * G, BF16) for _ in range(2)]
    tmp_silu = [cd.take(512, F32) for _ in range(2)]
    assert max(b1b2_top, cc_.off, cd.off) <= AR_BYTES

    pb = [nc.alloc_psum_tensor(f"pb{i}", [128, 512], F32).ap() for i in range(8)]

    def pbf(i, n=128):
        return pb[i].bitcast(BF16)[:, 0:n]

    sch = Sched()

    class Ring:
        def __init__(self, eng, base, n):
            self.eng, self.n = eng, n
            self.names = [f"{base}{i}" for i in range(n)]
            self.i = 0

        def dma(self, dst, src, wait=()):
            nm = self.names[self.i]
            self.i = (self.i + 1) % self.n
            w = list(wait)
            if sch.cur(nm) > 0:
                w.append((nm, sch.cur(nm)))
            post = sch.add(self.eng, lambda e, d=dst, s=src:
                           e.dma_start(out=d, in_=s), waits=w, inc=(nm, 16))
            return (nm, post)

    ring_ld = Ring("sy", "ld", 10)
    ring_st = Ring("sy", "st", 10)
    ring_gw = Ring("gp", "gw", 8)

    def dma_sy(dst, src, wait=(), tag="st"):
        return (ring_ld if tag == "ld" else ring_st).dma(dst, src, wait)

    def dma_gp(dst, src, wait=()):
        return ring_gw.dma(dst, src, wait)

    def mm(out, lhsT, rhs, start, stop, wait=(), inc=None):
        return sch.add("pe", lambda e, o=out, l=lhsT, r=rhs, st=start, sp=stop:
                       e.matmul(o, l, r, start=st, stop=sp), waits=wait, inc=inc)

    def tr(out, in_, wait=(), inc=None):
        return sch.add("pe", lambda e, o=out, i=in_: e.transpose(o, i, ident),
                       waits=wait, inc=inc)

    def ve(fn, wait=(), inc=VE_, drain=False):
        return sch.add("ve", fn, waits=wait, inc=inc, drain=drain)

    def se(fn, wait=(), inc=SE_, drain=False):
        return sch.add("se", fn, waits=wait, inc=inc, drain=drain)

    bank_free = [[] for _ in range(8)]

    ld_wg = dma_gp(wg_sb.rearrange("p (c f) -> p c f", c=ECH),
                   wg_d.rearrange("(c p) f -> p c f", p=128))
    ld_id = dma_gp(ident, id_d)
    ld_qd = dma_sy(qd_sb, qd_d, tag="ld")
    ld_kd = dma_sy(kd_sb, kd_d, tag="ld")
    ld_rmsw = dma_sy(rmsw_sb, rmsw_d, tag="ld")
    ld_dmt = dma_sy(dmt_sb, dmt_d, tag="ld")
    ld_csc = dma_gp(cs_c[0:32, :], csc_d)
    ld_css = dma_gp(cs_s[0:32, :], css_d)
    ve(lambda e: e.memset(vecs2[0][:, NEGHALF:NEGHALF + 1], -0.5))
    ve(lambda e: e.memset(vecs2[1][:, NEGHALF:NEGHALF + 1], -0.5))

    stA = {"stats": {}, "xntm": {}, "ln": {}, "logit": {}, "pexp": {},
           "negmax": {}, "xnst": {}}

    def emit_A_tile(t):
        bi = t % 2
        vecs, bnst = vecs2[bi], bnst2[bi]
        gt32, gt8 = gt32_2[bi], gt8_2[bi]

        def vcol(i):
            return vecs[:, i:i + 1]

        w = []
        if t >= 2:
            w = [(VE_, stA["stats"][t - 2]), (VE_, stA["xntm"][t - 2])]
        ld_x = dma_sy(xbuf[bi], x_d[t * 128:(t + 1) * 128, :], wait=w, tag="ld")

        wld = [ld_x]
        if t >= 2:
            wld.append((SE_, stA["pexp"][t - 2]))
        for i4 in range(4):
            ve(lambda e, x=xbuf[bi], i=i4: e.bn_stats(
                bnst[:, i * 6:(i + 1) * 6], x[:, i * 512:(i + 1) * 512]),
                wait=(wld if i4 == 0 else ()))
        stA["stats"][t] = ve(lambda e: e.bn_aggr(vecs[:, MEANV:MEANV + 2],
                                                 bnst[:]), drain=True)
        v_var = ve(lambda e: e.tensor_scalar(vcol(VARV), vcol(MSQ), 1.0, EPS,
                                             ALU.mult, ALU.add), drain=True)
        stA["ln"][t] = se(lambda e: e.activation(vcol(LNV), vcol(VARV), AF.Ln),
                          wait=[(VE_, v_var)])
        s_rstd = se(lambda e: e.activation(vcol(RSTD), vcol(LNV), AF.Exp,
                                           scale=vcol(NEGHALF)), drain=True)
        v_nmr = ve(lambda e: e.tensor_scalar(
            vcol(NMR), vcol(MEANV), vcol(RSTD), -1.0, ALU.mult, ALU.mult),
            wait=[(SE_, s_rstd)], drain=True)
        w = [(VE_, v_nmr)]
        if t >= 1:
            w.append((PE_, stA["logit"][t - 1]))
        stA["xntm"][t] = ve(lambda e, x=xbuf[bi]: e.tensor_scalar(
            xn_tm, x, vcol(RSTD), vcol(NMR), ALU.mult, ALU.add),
            wait=w, drain=True)

        last_copy = None
        for ec in range(ECH):
            bk = 6 + ec % 2
            wt = [(VE_, stA["xntm"][t])] + bank_free[bk]
            if t == 0 and ec == 0:
                wt.append(ld_id)
            p = tr(pbf(bk), xn_tm[:, ec * 128:(ec + 1) * 128], wait=wt, inc=PE_)
            wv = [(PE_, p)]
            if t >= 2 and ec == 0:
                wv += [stA["xnst"][t - 2], (PE_, stA["logit"][t - 2])]
            last_copy = se(lambda e, d=xnf[bi][:, ec * 128:(ec + 1) * 128],
                           s=pbf(bk): e.activation(d, s, AF.Copy), wait=wv)
            bank_free[bk] = [(SE_, last_copy)]

        stA["xnst"][t] = dma_sy(
            xn_sc[:, :, t * 128:(t + 1) * 128].rearrange("c p f -> p c f"),
            xnf[bi].rearrange("p (c f) -> p c f", c=ECH),
            wait=[(SE_, last_copy)])

        for ec in range(ECH):
            wt = []
            if ec == 0:
                wt = [(SE_, last_copy), ld_wg]
                if t >= 1:
                    wt += [(VE_, stA["negmax"][t - 1]), (SE_, stA["pexp"][t - 1])]
            stA["logit"][t] = mm(pb[5][:, 0:NE],
                                 xnf[bi][:, ec * 128:(ec + 1) * 128],
                                 wg_sb[:, ec * NE:(ec + 1) * NE],
                                 start=(ec == 0), stop=(ec == ECH - 1), wait=wt,
                                 inc=(PE_ if ec == ECH - 1 else None))

        pexp, cmb = gt32[:, 0:32], gt32[:, 32:64]
        top8, cum = gt8[:, 0:8], gt8[:, 8:16]
        keep, kept = gt8[:, 16:24], gt8[:, 24:32]
        tbig, t2c = gt8[:, 32:40], gt8[:, 40:48]
        stA["negmax"][t] = ve(lambda e: e.reduce_max(
            vcol(NEGMAX), pb[5][:, 0:NE], axis=AX.X, negate=True),
            wait=[(PE_, stA["logit"][t])], drain=True)
        stA["pexp"][t] = se(lambda e, p_=pexp: e.activation(
            p_, pb[5][:, 0:NE], AF.Exp, bias=vcol(NEGMAX)),
            wait=[(VE_, stA["negmax"][t])], drain=True)
        ve(lambda e, o=top8, i=pexp: e.max(out=o, in_=i),
           wait=[(SE_, stA["pexp"][t])], drain=True)
        ve(lambda e, o=cum, i=top8: e.tensor_tensor_scan(o, i, i, 0.0,
                                                         ALU.add, ALU.bypass),
           drain=True)
        ve(lambda e, i=cum: e.tensor_scalar(vcol(THRV), i[:, 7:8], 0.8, None,
                                            ALU.mult), drain=True)
        ve(lambda e, o=keep: e.memset(o[:, 0:1], 1.0))
        ve(lambda e, o=keep, i=cum: e.tensor_scalar(o[:, 1:8], i[:, 0:7],
                                                    vcol(THRV), None, ALU.is_lt),
           drain=True)
        ve(lambda e, o=kept, a=top8, b=keep: e.tensor_tensor(o, a, b, ALU.mult),
           drain=True)
        ve(lambda e, i=kept: e.tensor_reduce(vcol(DENOM), i, axis=AX.X,
                                             op=ALU.add), drain=True)
        ve(lambda e: e.reciprocal(vcol(RDEN), vcol(DENOM)), drain=True)
        ve(lambda e, o=tbig, i=keep: e.tensor_scalar(o, i, -BIG, BIG,
                                                     ALU.mult, ALU.add))
        ve(lambda e, o=t2c, a=tbig, b=kept: e.tensor_tensor(o, a, b, ALU.add),
           drain=True)
        ve(lambda e, i=t2c: e.tensor_reduce(vcol(BMIN), i, axis=AX.X,
                                            op=ALU.min), drain=True)
        ve(lambda e, o=cmb, i=pexp: e.scalar_tensor_tensor(
            o, i, vcol(BMIN), i, ALU.is_ge, ALU.mult), drain=True)
        ve(lambda e, d=combine[:, t * NE:(t + 1) * NE], i=cmb:
           e.tensor_scalar(d, i, vcol(RDEN), None, ALU.mult), drain=True)

    stB1 = {"wsl": {}, "proj": {}, "xntf": {}, "st": {}, "ld": {}}

    def emit_B1_wload(h):
        for nme, wdd in (("q", wq_d), ("k", wk_d), ("v", wv_d)):
            w = []
            if h >= 2:
                w = [(PE_, stB1["proj"][h - 2])]
            stB1["wsl"][(h, nme)] = dma_gp(
                wsl[nme][h % 2].rearrange("p (c f) -> p c f", c=ECH),
                wdd[:, h * 128:(h + 1) * 128].rearrange("(c p) f -> p c f", p=128),
                wait=w)

    def emit_B1_ld(m):
        n = m % NB1
        xi = m % 3
        w = []
        if m >= 3:
            w = [(PE_, stB1["xntf"][m - 3])]
        w += [stA["xnst"][t] for t in range(4 * n, 4 * n + 4)]
        stB1["ld"][m] = dma_sy(
            xnt[xi].rearrange("p (c f) -> p c f", c=ECH),
            xn_sc[:, :, n * NTILE:(n + 1) * NTILE].rearrange("c p f -> p c f"),
            wait=w, tag="ld")

    def emit_B1_tile(h, n):
        m = h * NB1 + n
        bi = m % 2
        xi = m % 3
        ld_t = stB1["ld"].pop(m)
        sl = slice(n * NTILE, (n + 1) * NTILE)
        ev_mm = {}
        rot_done = {}
        for nme, bk in (("q", 0), ("k", 1), ("v", 0)):
            for ec in range(ECH):
                wt = []
                if ec == 0:
                    wt = [stB1["wsl"][(h, nme)], ld_t] + bank_free[bk]
                ev_mm[nme] = mm(pb[bk], wsl[nme][h % 2][:, ec * 128:(ec + 1) * 128],
                                xnt[xi][:, ec * NTILE:(ec + 1) * NTILE],
                                start=(ec == 0), stop=(ec == ECH - 1),
                                wait=wt, inc=(PE_ if ec == ECH - 1 else None))
            dst = {"q": qrot_t, "k": krot_t, "v": vfm_t}[nme]
            wse = [(PE_, ev_mm[nme])]
            if m >= 2:
                wse += stB1["st"][m - 2]
            s_cp = se(lambda e, d=dst[bi], s=pb[bk]: e.activation(d, s, AF.Copy),
                      wait=wse)
            bank_free[bk] = [(SE_, s_cp)]
            ev_mm[nme + "cp"] = s_cp
            if nme == "v":
                continue
            ps = pb[bk]
            cosA = cs_c[0:32, sl]
            sinA = cs_s[0:32, sl]
            ve(lambda e, o=rt[1][0:32, :], a=ps[64:96, :], b=sinA:
               e.tensor_tensor(o, a, b, ALU.mult),
               wait=[(SE_, s_cp), ld_csc, ld_css], drain=True)
            v_c = ve(lambda e, o=rt[2][0:32, :], a=ps[64:96, :], b=cosA:
                     e.tensor_tensor(o, a, b, ALU.mult))
            bank_free[bk] = [(SE_, s_cp), (VE_, v_c)]
            ve(lambda e, o=rt[0][0:32, :], a=dst[bi][0:32, :], b=cosA:
               e.tensor_tensor(o, a, b, ALU.mult))
            ve(lambda e, o=rt[3][0:32, :], a=dst[bi][0:32, :], b=sinA:
               e.tensor_tensor(o, a, b, ALU.mult))
            ve(lambda e, o=dst[bi][0:32, :], a=rt[0][0:32, :], b=rt[1][0:32, :]:
               e.tensor_tensor(o, a, b, ALU.add), drain=True)
            rot_done[nme] = ve(
                lambda e, o=dst[bi][64:96, :], a=rt[2][0:32, :], b=rt[3][0:32, :]:
                e.tensor_tensor(o, a, b, ALU.subtract))
        stB1["proj"][h] = ev_mm["v"]
        stB1["xntf"][m] = ev_mm["v"]

        ev_ktm = ev_vtm = None
        for cc in range(4):
            wt = [(VE_, rot_done["k"])] + bank_free[1]
            p_tk = tr(pbf(1), krot_t[bi][:, cc * 128:(cc + 1) * 128], wait=wt,
                      inc=PE_)
            wt2 = [(SE_, ev_mm["vcp"])] + bank_free[0]
            p_tv = tr(pbf(0), vfm_t[bi][:, cc * 128:(cc + 1) * 128], wait=wt2,
                      inc=PE_)
            wk_ = [(PE_, p_tk), ld_kd]
            if m >= 2:
                wk_ += stB1["st"][m - 2]
            ev_ktm = ve(lambda e, o=ktm_t[bi][:, cc * 128:(cc + 1) * 128],
                        s=pbf(1), hh=h:
                        e.tensor_scalar(o, s, kd_sb[:, hh:hh + 1], None, ALU.mult),
                        wait=wk_)
            bank_free[1] = [(VE_, ev_ktm)]
            wv_ = [(PE_, p_tv)]
            if m >= 2:
                wv_ += stB1["st"][m - 2]
            ev_vtm = ve(lambda e, o=vtm_t[bi][:, cc * 128:(cc + 1) * 128],
                        s=pbf(0): e.tensor_copy(o, s), wait=wv_)
            bank_free[0] = [(VE_, ev_vtm)]

        sts = []
        sts.append(dma_sy(qk_sc[0, h, :, sl], qrot_t[bi],
                          wait=[(VE_, rot_done["q"])]))
        sts.append(dma_sy(qk_sc[1, h, :, sl], krot_t[bi],
                          wait=[(VE_, rot_done["k"])]))
        sts.append(dma_sy(tm_sc[0, h, n * NTILE:(n + 1) * NTILE, :]
                          .rearrange("(b s) d -> s b d", b=4),
                          ktm_t[bi].rearrange("p (b d) -> p b d", b=4),
                          wait=[(VE_, ev_ktm)]))
        sts.append(dma_sy(tm_sc[1, h, n * NTILE:(n + 1) * NTILE, :]
                          .rearrange("(b s) d -> s b d", b=4),
                          vtm_t[bi].rearrange("p (b d) -> p b d", b=4),
                          wait=[(VE_, ev_vtm)]))
        stB1["st"][m] = sts

    b2_prev_kvbf = [None, None]
    b2_sbi_guard = {}
    b2_attn_guard = {}

    class B2Pair:
        def __init__(self, p):
            self.p = p
            self.c = 0
            self.attn_st = {}
            self.slice_ld = {}
            self.attnev = [None, None]
            self.load_slice(0)
            self.load_slice(1)
            self.load_slice(2)
            wkv = [(PE_, sch.cur(PE_))]
            wkv += [x for x in b2_prev_kvbf if x is not None]
            ve(lambda e: e.memset(kv_sb[0], 0.0), wait=wkv, drain=True)
            ve(lambda e: e.memset(kv_sb[1], 0.0))
            ve(lambda e: e.memset(kv_bf[0], 0.0))
            v3 = ve(lambda e: e.memset(kv_bf[1], 0.0))
            self.kvbf = [(VE_, v3), (VE_, v3)]

        def load_slice(self, s):
            if s >= NSL:
                return
            sbi = s % 3
            evs = []
            w = []
            if sbi in b2_sbi_guard:
                w = [(PE_, b2_sbi_guard[sbi])]
            for hp in range(2):
                h = self.p * 2 + hp
                t0 = (s * SLICE) // NTILE
                for tt in range(t0, t0 + SLICE // NTILE):
                    w += stB1["st"][h * NB1 + tt]
                s0 = s * SLICE
                evs.append(dma_sy(b2buf[hp][0][sbi],
                                  qk_sc[0, h, :, s0:s0 + SLICE], wait=w, tag="ld"))
                evs.append(dma_sy(b2buf[hp][1][sbi],
                                  qk_sc[1, h, :, s0:s0 + SLICE], wait=w, tag="ld"))
                evs.append(dma_sy(
                    b2buf[hp][2][sbi].rearrange("p (b d) -> p b d", b=CPS),
                    tm_sc[0, h, s0:s0 + SLICE, :]
                    .rearrange("(b s) d -> s b d", b=CPS), wait=w, tag="ld"))
                evs.append(dma_sy(
                    b2buf[hp][3][sbi].rearrange("p (b d) -> p b d", b=CPS),
                    tm_sc[1, h, s0:s0 + SLICE, :]
                    .rearrange("(b s) d -> s b d", b=CPS), wait=w, tag="ld"))
                w = []
            self.slice_ld[s] = evs

        def step(self):
            c = self.c
            if c >= NCH:
                return False
            sbi = (c // CPS) % 3
            cc = c % CPS
            if cc == 0 and c > 0:
                self.load_slice(c // CPS + 2)
            last_kv = 0
            for hp in range(2):
                h = self.p * 2 + hp
                ba, bb, bc = 2 + 3 * hp, 3 + 3 * hp, 4 + 3 * hp
                qs = b2buf[hp][0][sbi][:, cc * 128:(cc + 1) * 128]
                ks = b2buf[hp][1][sbi][:, cc * 128:(cc + 1) * 128]
                kts = b2buf[hp][2][sbi][:, cc * 128:(cc + 1) * 128]
                vts = b2buf[hp][3][sbi][:, cc * 128:(cc + 1) * 128]
                wt = list(self.slice_ld[c // CPS]) + bank_free[ba]
                if c == 0:
                    wt.append(ld_dmt)
                p_sc = mm(pb[ba][:, 0:128], ks, qs, start=True, stop=True,
                          wait=wt, inc=PE_)
                v_mask = ve(lambda e, o=sc_bf[hp], s=pb[ba][:, 0:128], hh=h:
                            e.tensor_tensor(o, s,
                                            dmt_sb[:, hh * 128:(hh + 1) * 128],
                                            ALU.mult), wait=[(PE_, p_sc)])
                p_ia = mm(pb[ba][:, 0:128], sc_bf[hp], vts, start=True, stop=True,
                          wait=[(VE_, v_mask)], inc=PE_)
                wt = [self.kvbf[hp]] + bank_free[bb]
                p_int = mm(pb[bb][:, 0:128], qs, kv_bf[hp], start=True, stop=True,
                           wait=wt, inc=PE_)
                ws = [(PE_, p_int), ld_qd]
                if self.attnev[hp] is not None:
                    ws.append((VE_, self.attnev[hp]))
                s_tint = se(lambda e, o=tmp_int[hp], hh=h, s=pb[bb][:, 0:128]:
                            e.activation(o, s, AF.Copy,
                                         scale=qd_sb[:, hh:hh + 1]), wait=ws)
                bank_free[bb] = [(SE_, s_tint)]
                p_kv = mm(pb[bc][:, 0:128], kts, vts, start=True, stop=True,
                          wait=list(bank_free[bc]), inc=PE_)
                last_kv = p_kv
                v_kvu = ve(lambda e, rc=rC[h], hh=hp, s=pb[bc][:, 0:128]:
                           e.scalar_tensor_tensor(kv_sb[hh], kv_sb[hh], rc, s,
                                                  ALU.mult, ALU.add),
                           wait=[(PE_, p_kv), self.kvbf[hp]], drain=True)
                bank_free[bc] = [(VE_, v_kvu)]
                s_kvbf = se(lambda e, hh=hp: e.activation(kv_bf[hh], kv_sb[hh],
                                                          AF.Copy),
                            wait=[(VE_, v_kvu), (PE_, p_int)], drain=True)
                self.kvbf[hp] = (SE_, s_kvbf)

                grp = c // 4
                abuf = attn_ch[hp][grp % 2]
                wt = [(SE_, s_tint), (PE_, p_ia)]
                if c % 4 == 0 and (hp, grp % 2) in b2_attn_guard:
                    wt.append(b2_attn_guard[(hp, grp % 2)])
                v_att = ve(lambda e, o=abuf[:, (c % 4) * 128:(c % 4 + 1) * 128],
                           t_=tmp_int[hp], s=pb[ba][:, 0:128]:
                           e.tensor_tensor(o, t_, s, ALU.add), wait=wt)
                bank_free[ba] = [(VE_, v_att)]
                self.attnev[hp] = v_att
                if c % 4 == 3:
                    b2_attn_guard[(hp, grp % 2)] = dma_sy(
                        attn_sc[h, (c - 3) * 128:(c + 1) * 128, :]
                        .rearrange("(b s) d -> s b d", b=4),
                        abuf.rearrange("p (b d) -> p b d", b=4),
                        wait=[(VE_, v_att)])
            if cc == CPS - 1:
                b2_sbi_guard[sbi] = last_kv
            self.c += 1
            if self.c == NCH:
                b2_prev_kvbf[0] = self.kvbf[0]
                b2_prev_kvbf[1] = self.kvbf[1]
            return True

    b2_pending = []
    cur_pair = [None]

    def b2_steps(k):
        for _ in range(k):
            if cur_pair[0] is None or cur_pair[0].c >= NCH:
                if b2_pending:
                    cur_pair[0] = B2Pair(b2_pending.pop(0))
                else:
                    return
            cur_pair[0].step()

    emit_B1_wload(0)
    b1_next = 0
    for t in range(NT):
        emit_A_tile(t)
        if t % 4 == 3:
            emit_B1_ld(b1_next)
            emit_B1_tile(0, b1_next)
            b1_next += 1
    assert b1_next == NB1
    ld_next = NB1
    for h in range(1, H):
        emit_B1_wload(h)
        for n in range(NB1):
            m = h * NB1 + n
            while ld_next <= min(m + 2, H * NB1 - 1):
                emit_B1_ld(ld_next)
                ld_next += 1
            emit_B1_tile(h, n)
            b2_steps(2)
        if h % 2 == 1:
            b2_pending.append(h // 2)
    wo_hoist_w = [(PE_, stB1["proj"][H - 1])]
    wo_hoist_w += [(nm, sch.cur(nm)) for nm in ring_ld.names if sch.cur(nm) > 0]
    wo_hoist_w += [(nm, sch.cur(nm)) for nm in ring_gw.names if sch.cur(nm) > 0]
    ev_wo = []
    for c0 in range(0, ECH, 4):
        ev_wo.append(dma_gp(
            wo_sb.rearrange("p (c f) -> p c f", c=ECH)[:, c0:c0 + 4, :],
            wo_d[c0 * 128:(c0 + 4) * 128, :].rearrange("(c p) f -> p c f", p=128),
            wait=wo_hoist_w))
        wo_hoist_w = []

    while b2_pending or (cur_pair[0] is not None and cur_pair[0].c < NCH):
        b2_steps(8)

    sch.barrier()
    bank_free = [[] for _ in range(8)]

    v_wos = None
    for ec in range(ECH):
        v_wos = ve(lambda e, d=wo_sb[:, ec * E:(ec + 1) * E], ecc=ec:
                   e.tensor_scalar(d, d, rmsw_sb[:, ecc:ecc + 1], None, ALU.mult),
                   wait=ev_wo + [ld_rmsw])

    stC = {"ain": {}, "xld": {}, "sumsq": {}, "attnn": {}, "ln": {}, "tp": {},
           "womm": {}, "xat": {}, "xatst": {}}

    def emit_C_ld(t):
        bi = t % 2
        w = []
        if t >= 2:
            w = [(VE_, stC["sumsq"][t - 2]), (SE_, stC["attnn"][t - 2])]
        stC["ain"][t] = dma_sy(
            attn_in[bi].rearrange("p (h d) -> p h d", h=H),
            attn_sc[:, t * 128:(t + 1) * 128, :].rearrange("h s d -> s h d"),
            wait=w, tag="ld")
        w = []
        if t >= 2:
            w = [stC["xatst"][t - 2]]
        stC["xld"][t] = dma_sy(xbuf[bi], x_d[t * 128:(t + 1) * 128, :], wait=w,
                               tag="ld")

    def emit_C_tile(t):
        bi = t % 2
        vecs = vecs2[bi]

        def vcol(i):
            return vecs[:, i:i + 1]

        ld_xc = stC["xld"].pop(t)
        wv = [stC["ain"][t]]
        if t >= 2:
            wv.append((SE_, stC["attnn"][t - 2]))
        stC["sumsq"][t] = ve(lambda e, a=attn_in[bi]: e.scalar_tensor_tensor(
            trash, a, 1.0, a, ALU.mult, ALU.mult, accum_out=vcol(SUMSQ)),
            wait=wv, drain=True)
        v_var = ve(lambda e: e.tensor_scalar(vcol(VARV), vcol(SUMSQ), INV_E, EPS,
                                             ALU.mult, ALU.add), drain=True)
        stC["ln"][t] = se(lambda e: e.activation(vcol(LNV), vcol(VARV), AF.Ln),
                          wait=[(VE_, v_var)])
        s_rstd = se(lambda e: e.activation(vcol(RSTD), vcol(LNV), AF.Exp,
                                           scale=vcol(NEGHALF)), drain=True)
        w = [(SE_, s_rstd)]
        if t >= 1:
            w.append((PE_, stC["tp"][t - 1]))
        stC["attnn"][t] = se(lambda e, a=attn_in[bi]: e.activation(
            attnn_tm, a, AF.Copy, scale=vcol(RSTD)), wait=w, drain=True)

        last_copy = None
        for ec in range(ECH):
            bk = 4 + ec % 2
            wt = [(SE_, stC["attnn"][t])] + bank_free[bk]
            stC["tp"][t] = tr(pbf(bk), attnn_tm[:, ec * 128:(ec + 1) * 128],
                              wait=wt, inc=PE_)
            wv = [(PE_, stC["tp"][t])]
            if ec == 0 and t >= 2:
                wv.append((PE_, stC["womm"][t - 2]))
            last_copy = ve(lambda e, d=attnnf[bi][:, ec * 128:(ec + 1) * 128],
                           s=pbf(bk): e.tensor_copy(d, s), wait=wv)
            bank_free[bk] = [(VE_, last_copy)]

        for ot in range(4):
            for ec in range(ECH):
                wt = []
                if ec == 0:
                    wt = [(VE_, last_copy), (VE_, v_wos)] + bank_free[ot]
                stC["womm"][t] = mm(pb[ot], attnnf[bi][:, ec * 128:(ec + 1) * 128],
                                    wo_sb[:, ec * E + ot * 512:
                                          ec * E + ot * 512 + 512],
                                    start=(ec == 0), stop=(ec == ECH - 1),
                                    wait=wt, inc=(PE_ if ec == ECH - 1 else None))
            stC["xat"][t] = ve(lambda e, o=xbuf[bi][:, ot * 512:(ot + 1) * 512],
                               s=pb[ot]: e.tensor_tensor(o, s, o, ALU.add),
                               wait=[(PE_, stC["womm"][t]), ld_xc])
            bank_free[ot] = [(VE_, stC["xat"][t])]
        stC["xatst"][t] = dma_sy(xat_sc[t * 128:(t + 1) * 128, :], xbuf[bi],
                                 wait=[(VE_, stC["xat"][t])])

    emit_C_ld(0)
    for t in range(NT):
        if t + 1 < NT:
            emit_C_ld(t + 1)
        emit_C_tile(t)

    sch.barrier(engines=("pe", "ve", "se", "sy"))
    bank_free = [[] for _ in range(8)]

    ev_wgu = {}
    ev_wd = {}
    pair_mm_done = {}
    wd_mm_done = {}
    ev_down_mm = {}
    ev_acc_ev = {}
    ev_y_st = {}
    ev_xng = {}
    ev_accinit = {}
    down_all = {}
    hm_ev = {}
    tsu = {}
    N_PAIR = NG * NE * ICH
    N_WD = NG * NE * 4

    def emit_wgu(pr, wait=()):
        e2 = (pr // ICH) % NE
        jj2 = pr % ICH
        ev_wgu[pr] = dma_gp(
            wgu_sl[pr % 2].rearrange("p (c f) -> p c f", c=ECH),
            wgu_d[e2, :, jj2 * 256:(jj2 + 1) * 256]
            .rearrange("(c p) f -> p c f", p=128), wait=wait)

    def emit_wd(n2, wait=()):
        e2 = (n2 // 4) % NE
        ot2 = n2 % 4
        ev_wd[n2] = dma_gp(
            wd_sl[n2 % 2].rearrange("p (c f) -> p c f", c=ICH),
            wd_d[e2, :, ot2 * 512:(ot2 + 1) * 512]
            .rearrange("(c p) f -> p c f", p=128), wait=wait)

    emit_wgu(0)
    emit_wgu(1)
    emit_wd(0)
    emit_wd(1)
    last_gu_mm_of_group = 0

    def emit_y_st(g):
        for tl in range(TPG):
            t0 = g * G + tl * 128
            ev_y_st[(g, tl)] = dma_sy(y_d[t0:t0 + 128, :],
                                      acc[:, tl * E:(tl + 1) * E],
                                      wait=[(VE_, ev_acc_ev[(g, tl)])])

    for g in range(NG):
        ev_xng[g] = {}
        for ec_ in range(ECH):
            w = []
            if g >= 1 and ec_ == 0:
                w = [(PE_, last_gu_mm_of_group)]
            ev_xng[g][ec_] = dma_sy(
                xn_g[:, ec_ * G:(ec_ + 1) * G],
                xn_sc[ec_, :, g * G:(g + 1) * G],
                wait=w, tag="ld")
        if g >= 1:
            emit_y_st(g - 1)
        for tl in range(TPG):
            t0 = g * G + tl * 128
            w = []
            if g >= 1:
                w = [ev_y_st[(g - 1, tl)]]
            ev_accinit[(g, tl)] = dma_sy(acc[:, tl * E:(tl + 1) * E],
                                         xat_sc[t0:t0 + 128, :], wait=w, tag="ld")

        for e_ in range(NE):
            eg = g * NE + e_
            hb = eg % 2
            for jj in range(ICH):
                pr = eg * ICH + jj
                for nt in range(NTG):
                    k = jj * NTG + nt
                    gA, gB = 2 * (k % 2), 2 * (k % 2) + 1
                    mmg = mmu = None
                    for half, bk in ((0, gA), (1, gB)):
                        for ec in range(ECH):
                            wt = [ev_xng[g][ec]]
                            if ec == 0:
                                wt += [ev_wgu[pr]] + bank_free[bk]
                            r = mm(pb[bk],
                                   wgu_sl[pr % 2][:, ec * 256 + half * 128:
                                                  ec * 256 + half * 128 + 128],
                                   xn_g[:, ec * G + nt * 512:
                                        ec * G + nt * 512 + 512],
                                   start=(ec == 0), stop=(ec == ECH - 1),
                                   wait=wt,
                                   inc=(PE_ if ec == ECH - 1 else None))
                            if half == 0:
                                mmg = r
                            else:
                                mmu = r
                    wsi = [(PE_, mmg)]
                    if k % 2 in tsu:
                        wsi.append((VE_, tsu[k % 2]))
                    s_si = se(lambda e, o=tmp_silu[k % 2], s=pb[gA]:
                              e.activation(o, s, AF.Silu), wait=wsi)
                    wv = [(SE_, s_si), (PE_, mmu)]
                    if jj == 0 and nt == 0 and eg >= 2:
                        wv.append((PE_, down_all[eg - 2]))
                    v_h = ve(lambda e, o=h_sb[hb][:, jj * G + nt * 512:
                                                  jj * G + nt * 512 + 512],
                             si=tmp_silu[k % 2], u=pb[gB]:
                             e.tensor_tensor(o, si, u, ALU.mult), wait=wv)
                    hm_ev[(eg, jj, nt)] = v_h
                    tsu[k % 2] = v_h
                    bank_free[gA] = [(SE_, s_si)]
                    bank_free[gB] = [(VE_, v_h)]
                pair_mm_done[pr] = mmu
                if pr + 2 < N_PAIR:
                    emit_wgu(pr + 2, wait=[(PE_, pair_mm_done[pr])])

            first_evac = True
            v_ev = 0
            for ot in range(4):
                n2 = eg * 4 + ot
                for tl in range(TPG):
                    bk = 4 + (ot * TPG + tl) % 4
                    grp = ot * TPG + tl
                    ntt = (tl * 128) // 512
                    for kc in range(ICH):
                        wt = [(VE_, hm_ev[(eg, kc, ntt)])]
                        if kc == 0:
                            wt += [ev_wd[n2]] + bank_free[bk]
                        ev_down_mm[(eg, grp)] = mm(
                            pb[bk],
                            h_sb[hb][:, kc * G + tl * 128:kc * G + tl * 128 + 128],
                            wd_sl[n2 % 2][:, kc * 512:(kc + 1) * 512],
                            start=(kc == 0), stop=(kc == ICH - 1), wait=wt,
                            inc=(PE_ if kc == ICH - 1 else None))
                    ce = combine[:, (g * TPG + tl) * NE + e_:
                                 (g * TPG + tl) * NE + e_ + 1]
                    wv = [(PE_, ev_down_mm[(eg, grp)]), ev_accinit[(g, tl)]]
                    v_ev = ve(lambda e, p=pb[bk], c=ce,
                              a=acc[:, tl * E + ot * 512:tl * E + ot * 512 + 512]:
                              e.scalar_tensor_tensor(a, p, c, a, ALU.mult,
                                                     ALU.add),
                              wait=wv, drain=first_evac)
                    first_evac = False
                    bank_free[bk] = [(VE_, v_ev)]
                    ev_acc_ev[(g, tl)] = v_ev
                wd_mm_done[n2] = ev_down_mm[(eg, ot * TPG + TPG - 1)]
                if n2 + 2 < N_WD:
                    emit_wd(n2 + 2, wait=[(PE_, wd_mm_done[n2])])
            down_all[eg] = ev_down_mm[(eg, TPG * 4 - 1)]
        last_gu_mm_of_group = pair_mm_done[(g * NE + NE - 1) * ICH + ICH - 1]
    emit_y_st(NG - 1)

    sem_names = [PE_, VE_, SE_] + ring_ld.names + ring_st.names + ring_gw.names
    sems = {name: nc.alloc_semaphore(name) for name in sem_names}

    with nc.Block() as block:
        def runner(engname):
            def run(eng):
                waited = {}
                for op_eng, fn, waits, incs, drain in sch.ops:
                    if op_eng != engname:
                        continue
                    if drain:
                        eng.drain()
                    for s_, v in waits:
                        if v > 0 and v > waited.get(s_, 0):
                            eng.wait_ge(sems[s_], v)
                            waited[s_] = v
                    inst = fn(eng)
                    for s_, a in incs:
                        inst.then_inc(sems[s_], a)
                if engname == "sy":
                    for nm in ring_st.names:
                        if sch.cur(nm) > 0:
                            eng.wait_ge(sems[nm], sch.cur(nm))
            return run
        block.sync(runner("sy"))
        block.gpsimd(runner("gp"))
        block.tensor(runner("pe"))
        block.vector(runner("ve"))
        block.scalar(runner("se"))

    return nc


def _prep_inputs(L0_attn_x, L0_attn_Wq, L0_attn_Wk, L0_attn_Wv, L0_attn_Wo,
                 L0_attn_rms_w, L0_ffn_gate_weight, L0_ffn_gate_up_proj,
                 L0_ffn_down_proj):
    c32 = lambda a: np.ascontiguousarray(np.asarray(a, dtype=np.float32))
    x = c32(L0_attn_x)
    consts, _ = _consts()
    wgu = np.asarray(L0_ffn_gate_up_proj, np.float32).transpose(0, 2, 1)
    idx = np.concatenate([np.r_[jj * 128:(jj + 1) * 128,
                                (jj + ICH) * 128:(jj + ICH + 1) * 128]
                          for jj in range(ICH)])
    wgu = np.ascontiguousarray(wgu[:, :, idx])
    wd = np.ascontiguousarray(np.asarray(L0_ffn_down_proj, np.float32)
                              .transpose(0, 2, 1))
    base = {
        "wq": c32(L0_attn_Wq), "wk": c32(L0_attn_Wk), "wv": c32(L0_attn_Wv),
        "wo": c32(L0_attn_Wo),
        "wg": np.ascontiguousarray(np.asarray(L0_ffn_gate_weight, np.float32).T),
        "wgu": wgu, "wd": wd,
        "rmsw": np.ascontiguousarray(c32(L0_attn_rms_w).reshape(ECH, 128).T),
        "csc": consts["csc"], "css": consts["css"], "dmt": consts["dmt"],
        "qdc": consts["qd"], "kdc": consts["kd"], "idm": consts["ident"],
    }
    return x, base


def kernel(L0_attn_x, L0_attn_Wq, L0_attn_Wk, L0_attn_Wv, L0_attn_Wo,
           L0_attn_rms_w, L0_ffn_gate_weight, L0_ffn_gate_up_proj,
           L0_ffn_down_proj):
    x, base = _prep_inputs(L0_attn_x, L0_attn_Wq, L0_attn_Wk, L0_attn_Wv,
                           L0_attn_Wo, L0_attn_rms_w, L0_ffn_gate_weight,
                           L0_ffn_gate_up_proj, L0_ffn_down_proj)
    nc = build_nc()
    in_maps = [dict(base, x=np.ascontiguousarray(x[c])) for c in range(NCORES)]
    res = run_bass_kernel_spmd(nc, in_maps, core_ids=list(range(NCORES)))
    return np.stack([np.asarray(r["y"], np.float32) for r in res.results], axis=0)


# version 11
# speedup vs baseline: 1.1488x; 1.1488x over previous
import os
import numpy as np
import ml_dtypes
import concourse.bass as bass
import concourse.mybir as mybir
from concourse.bass_utils import run_bass_kernel_spmd

F32 = mybir.dt.float32
BF16 = mybir.dt.bfloat16
U8 = mybir.dt.uint8
AF = mybir.ActivationFunctionType
ALU = mybir.AluOpType
AX = mybir.AxisListType

S = int(os.environ.get("MK_S", "4096"))
NCORES = int(os.environ.get("MK_NCORES", "8"))
E, ECH = 2048, 16
H = 16
CH = 128
NCH = S // CH
NT = S // 128
NTILE = 512
NB1 = S // NTILE
NE, ICH, INTER = 32, 6, 768
G = min(1024, S)
NG = S // G
TPG = G // 128
NTG = G // 512
EPS = 1e-5
BIG = 1e30
INV_E = 1.0 / E
SLICE = min(512, S)
NSL = S // SLICE
CPS = SLICE // CH

(SUMV, SUMSQ, MEANV, MSQ, VARV, RSTD, NMR, NEGMAX, THRV, DENOM, RDEN, BMIN,
 LNV, NEGHALF) = range(14)

PE_, VE_, SE_ = "pe", "ve", "se"


def _consts():
    f = np.float64
    start = 2.0 ** (-2.0 ** (-(np.log2(H) - 3.0)))
    slopes = np.float32([start * start**i for i in range(H)]).astype(f)
    r = np.exp(-slopes)
    i = np.arange(CH, dtype=f)
    qd = (r[None, :] ** (i[:, None] + 1.0)).astype(np.float32)
    kd = (r[None, :] ** (CH - 1.0 - i[:, None])).astype(np.float32)
    diff = i[:, None] - i[None, :]
    dm = np.where(diff >= 0, r[:, None, None] ** np.where(diff >= 0, diff, 0.0)[None], 0.0)
    dmt = np.ascontiguousarray(dm.transpose(2, 0, 1).reshape(CH, H * CH).astype(np.float32))
    rC = [float(v) for v in (r ** float(CH))]
    freq = (1.0 / 1024.0) ** np.linspace(0.0, 1.0, 32, dtype=np.float32).astype(f)
    theta = np.arange(S, dtype=f)[:, None] * freq[None, :]
    csc = np.ascontiguousarray(np.cos(theta).T.astype(np.float32))
    css = np.ascontiguousarray(np.sin(theta).T.astype(np.float32))
    ident = np.eye(128, dtype=np.float32)
    return dict(qd=qd, kd=kd, dmt=dmt, csc=csc, css=css, ident=ident), rC


class Sched:
    def __init__(self):
        self.ops = []
        self.cnt = {}
        self.pending = {}

    def add(self, eng, fn, waits=(), inc=None, drain=False):
        w = list(waits)
        if eng in self.pending:
            w = self.pending.pop(eng) + w
        incs = []
        post = None
        if inc is not None:
            sem, amt = inc if isinstance(inc, tuple) else (inc, 1)
            self.cnt[sem] = self.cnt.get(sem, 0) + amt
            incs.append((sem, amt))
            post = self.cnt[sem]
        self.ops.append((eng, fn, w, incs, drain))
        return post

    def cur(self, sem):
        return self.cnt.get(sem, 0)

    def barrier(self, engines=("pe", "ve", "se", "sy", "gp")):
        snap = [(s, v) for s, v in self.cnt.items()]
        for e in engines:
            self.pending[e] = list(snap) + self.pending.get(e, [])


def build_nc():
    nc = bass.Bass("TRN2", target_bir_lowering=False)

    x_d = nc.dram_tensor("x", [S, E], F32, kind="ExternalInput").ap()
    wq_d = nc.dram_tensor("wq", [E, E], F32, kind="ExternalInput").ap()
    wk_d = nc.dram_tensor("wk", [E, E], F32, kind="ExternalInput").ap()
    wv_d = nc.dram_tensor("wv", [E, E], F32, kind="ExternalInput").ap()
    wo_d = nc.dram_tensor("wo", [E, E], F32, kind="ExternalInput").ap()
    wg_d = nc.dram_tensor("wg", [E, NE], F32, kind="ExternalInput").ap()
    wgu_d = nc.dram_tensor("wgu", [NE, E, 2 * INTER], F32, kind="ExternalInput").ap()
    wd_d = nc.dram_tensor("wd", [NE, INTER, E], F32, kind="ExternalInput").ap()
    rmsw_d = nc.dram_tensor("rmsw", [128, ECH], F32, kind="ExternalInput").ap()
    csc_d = nc.dram_tensor("csc", [32, S], F32, kind="ExternalInput").ap()
    css_d = nc.dram_tensor("css", [32, S], F32, kind="ExternalInput").ap()
    dmt_d = nc.dram_tensor("dmt", [128, H * 128], F32, kind="ExternalInput").ap()
    qd_d = nc.dram_tensor("qdc", [128, H], F32, kind="ExternalInput").ap()
    kd_d = nc.dram_tensor("kdc", [128, H], F32, kind="ExternalInput").ap()
    id_d = nc.dram_tensor("idm", [128, 128], F32, kind="ExternalInput").ap()
    y_d = nc.dram_tensor("y", [S, E], F32, kind="ExternalOutput").ap()

    xn_sc = nc.dram_tensor("xn_sc", [ECH, 128, S], BF16).ap()
    qk_sc = nc.dram_tensor("qk_sc", [2, H, 128, S], BF16).ap()
    tm_sc = nc.dram_tensor("tm_sc", [2, H, S, 128], BF16).ap()
    attn_sc = nc.dram_tensor("attn_sc", [H, S, 128], BF16).ap()
    xat_sc = nc.dram_tensor("xat_sc", [S, E], F32).ap()

    _, rC = _consts()

    def sb(name, shape, dt):
        return nc.alloc_sbuf_tensor(name, shape, dt).ap()

    ident = sb("ident", [128, 128], BF16)
    combine = sb("combine", [128, NT * NE], F32)
    xbuf = [sb(f"xbuf{i}", [128, E], F32) for i in range(2)]
    vecs2 = [sb(f"vecs{i}", [128, 16], F32) for i in range(2)]
    bnst2 = [sb(f"bnst{i}", [128, 24], F32) for i in range(2)]
    gt32_2 = [sb(f"gt32_{i}", [128, 64], F32) for i in range(2)]
    gt8_2 = [sb(f"gt8_{i}", [128, 48], F32) for i in range(2)]
    wg_sb = sb("wg_sb", [128, ECH * NE], BF16)
    wgu_sl = [sb(f"wgu_sl{i}", [128, ECH * 256], BF16) for i in range(2)]
    wd_sl = [sb(f"wdsl{i}", [128, ICH * 512], BF16) for i in range(2)]
    kv_sb = [sb(f"kv{i}", [128, 128], F32) for i in range(2)]
    kv_bf = [sb(f"kvbf{i}", [128, 128], BF16) for i in range(2)]
    qd_sb = sb("qdsb", [128, H], F32)
    kd_sb = sb("kdsb", [128, H], F32)
    rmsw_sb = sb("rmsw_sb", [128, ECH], F32)
    trash = sb("trash", [128, E], BF16)
    dmt_sb = sb("dmt_sb", [128, H * 128], F32)
    xn_tm = sb("xn_tm", [128, E], BF16)
    xnf = [sb(f"xnf{i}", [128, E], BF16) for i in range(2)]
    sc_bf = [sb(f"scbf{i}", [128, 128], BF16) for i in range(2)]
    tmp_int = [sb(f"tmpint{i}", [128, 128], F32) for i in range(2)]
    attn_ch = [[sb(f"attnch{i}_{j}", [128, 4 * 128], BF16) for j in range(2)]
               for i in range(2)]

    AR_BYTES = 126 * 1024
    arena = nc.alloc_sbuf_tensor("arena", [128, AR_BYTES], U8).ap()

    class Carver:
        def __init__(self):
            self.off = 0

        def take(self, nelem, dt):
            sz = {F32: 4, BF16: 2}[dt]
            self.off = (self.off + 63) & ~63
            ap = arena[:, self.off:self.off + nelem * sz].bitcast(dt)
            self.off += nelem * sz
            assert self.off <= AR_BYTES, f"arena overflow {self.off}"
            return ap

    cb = Carver()
    xnt = [cb.take(ECH * NTILE, BF16) for _ in range(3)]
    wsl = {n: [cb.take(ECH * 128, BF16) for _ in range(2)] for n in "qkv"}
    qrot_t = [cb.take(NTILE, BF16) for _ in range(2)]
    krot_t = [cb.take(NTILE, BF16) for _ in range(2)]
    vfm_t = [cb.take(NTILE, BF16) for _ in range(2)]
    ktm_t = [cb.take(NTILE, BF16) for _ in range(2)]
    vtm_t = [cb.take(NTILE, BF16) for _ in range(2)]
    rt = [cb.take(NTILE, BF16) for _ in range(4)]
    cs_c = cb.take(S, BF16)
    cs_s = cb.take(S, BF16)
    b2buf = [[[cb.take(SLICE, BF16) for _ in range(3)] for _ in range(4)]
             for _ in range(2)]
    b1b2_top = cb.off
    cc_ = Carver()
    wo_sb = cc_.take(ECH * E, BF16)
    attn_in = [cc_.take(E, BF16) for _ in range(2)]
    attnn_tm = cc_.take(E, BF16)
    attnnf = [cc_.take(ECH * 128, BF16) for _ in range(2)]
    cd = Carver()
    xn_g = cd.take(ECH * G, BF16)
    acc = cd.take(TPG * E, F32)
    h_sb = [cd.take(ICH * G, BF16) for _ in range(2)]
    tmp_silu = [cd.take(512, F32) for _ in range(2)]
    assert max(b1b2_top, cc_.off, cd.off) <= AR_BYTES

    pb = [nc.alloc_psum_tensor(f"pb{i}", [128, 512], F32).ap() for i in range(8)]

    def pbf(i, n=128):
        return pb[i].bitcast(BF16)[:, 0:n]

    sch = Sched()

    class Ring:
        def __init__(self, eng, base, n):
            self.eng, self.n = eng, n
            self.names = [f"{base}{i}" for i in range(n)]
            self.i = 0

        def dma(self, dst, src, wait=()):
            nm = self.names[self.i]
            self.i = (self.i + 1) % self.n
            w = list(wait)
            if sch.cur(nm) > 0:
                w.append((nm, sch.cur(nm)))
            post = sch.add(self.eng, lambda e, d=dst, s=src:
                           e.dma_start(out=d, in_=s), waits=w, inc=(nm, 16))
            return (nm, post)

    ring_ld = Ring("sy", "ld", 10)
    ring_st = Ring("sy", "st", 10)
    ring_gw = Ring("gp", "gw", 8)

    def dma_sy(dst, src, wait=(), tag="st"):
        return (ring_ld if tag == "ld" else ring_st).dma(dst, src, wait)

    def dma_gp(dst, src, wait=()):
        return ring_gw.dma(dst, src, wait)

    def mm(out, lhsT, rhs, start, stop, wait=(), inc=None):
        return sch.add("pe", lambda e, o=out, l=lhsT, r=rhs, st=start, sp=stop:
                       e.matmul(o, l, r, start=st, stop=sp), waits=wait, inc=inc)

    def tr(out, in_, wait=(), inc=None):
        return sch.add("pe", lambda e, o=out, i=in_: e.transpose(o, i, ident),
                       waits=wait, inc=inc)

    def ve(fn, wait=(), inc=VE_, drain=False):
        return sch.add("ve", fn, waits=wait, inc=inc, drain=drain)

    def se(fn, wait=(), inc=SE_, drain=False):
        return sch.add("se", fn, waits=wait, inc=inc, drain=drain)

    bank_free = [[] for _ in range(8)]

    ld_wg = dma_gp(wg_sb.rearrange("p (c f) -> p c f", c=ECH),
                   wg_d.rearrange("(c p) f -> p c f", p=128))
    ld_id = dma_gp(ident, id_d)
    ld_qd = dma_sy(qd_sb, qd_d, tag="ld")
    ld_kd = dma_sy(kd_sb, kd_d, tag="ld")
    ld_rmsw = dma_sy(rmsw_sb, rmsw_d, tag="ld")
    ld_dmt = dma_sy(dmt_sb, dmt_d, tag="ld")
    ld_csc = dma_gp(cs_c[0:32, :], csc_d)
    ld_css = dma_gp(cs_s[0:32, :], css_d)
    ve(lambda e: e.memset(vecs2[0][:, NEGHALF:NEGHALF + 1], -0.5))
    ve(lambda e: e.memset(vecs2[1][:, NEGHALF:NEGHALF + 1], -0.5))

    stA = {"stats": {}, "xntm": {}, "ln": {}, "logit": {}, "pexp": {},
           "negmax": {}, "xnst": {}, "xld": {}}

    def emit_A_ld(t):
        bi = t % 2
        w = []
        if t >= 2:
            w = [(VE_, stA["stats"][t - 2]), (VE_, stA["xntm"][t - 2])]
        stA["xld"][t] = dma_sy(xbuf[bi], x_d[t * 128:(t + 1) * 128, :], wait=w,
                               tag="ld")

    def emit_A_tile(t):
        bi = t % 2
        vecs, bnst = vecs2[bi], bnst2[bi]
        gt32, gt8 = gt32_2[bi], gt8_2[bi]

        def vcol(i):
            return vecs[:, i:i + 1]

        ld_x = stA["xld"].pop(t)
        wld = [ld_x]
        if t >= 2:
            wld.append((SE_, stA["pexp"][t - 2]))
        for i4 in range(4):
            ve(lambda e, x=xbuf[bi], i=i4: e.bn_stats(
                bnst[:, i * 6:(i + 1) * 6], x[:, i * 512:(i + 1) * 512]),
                wait=(wld if i4 == 0 else ()))
        stA["stats"][t] = ve(lambda e: e.bn_aggr(vecs[:, MEANV:MEANV + 2],
                                                 bnst[:]), drain=True)
        v_var = ve(lambda e: e.tensor_scalar(vcol(VARV), vcol(MSQ), 1.0, EPS,
                                             ALU.mult, ALU.add), drain=True)
        stA["ln"][t] = se(lambda e: e.activation(vcol(LNV), vcol(VARV), AF.Ln),
                          wait=[(VE_, v_var)])
        s_rstd = se(lambda e: e.activation(vcol(RSTD), vcol(LNV), AF.Exp,
                                           scale=vcol(NEGHALF)), drain=True)
        v_nmr = ve(lambda e: e.tensor_scalar(
            vcol(NMR), vcol(MEANV), vcol(RSTD), -1.0, ALU.mult, ALU.mult),
            wait=[(SE_, s_rstd)], drain=True)
        w = [(VE_, v_nmr)]
        if t >= 1:
            w.append((PE_, stA["logit"][t - 1]))
        stA["xntm"][t] = ve(lambda e, x=xbuf[bi]: e.tensor_scalar(
            xn_tm, x, vcol(RSTD), vcol(NMR), ALU.mult, ALU.add),
            wait=w, drain=True)

        last_copy = None
        for ec in range(ECH):
            bk = 6 + ec % 2
            wt = [(VE_, stA["xntm"][t])] + bank_free[bk]
            if t == 0 and ec == 0:
                wt.append(ld_id)
            p = tr(pbf(bk), xn_tm[:, ec * 128:(ec + 1) * 128], wait=wt, inc=PE_)
            wv = [(PE_, p)]
            if t >= 2 and ec == 0:
                wv += [stA["xnst"][t - 2], (PE_, stA["logit"][t - 2])]
            last_copy = se(lambda e, d=xnf[bi][:, ec * 128:(ec + 1) * 128],
                           s=pbf(bk): e.activation(d, s, AF.Copy), wait=wv)
            bank_free[bk] = [(SE_, last_copy)]

        stA["xnst"][t] = dma_sy(
            xn_sc[:, :, t * 128:(t + 1) * 128].rearrange("c p f -> p c f"),
            xnf[bi].rearrange("p (c f) -> p c f", c=ECH),
            wait=[(SE_, last_copy)])

        for ec in range(ECH):
            wt = []
            if ec == 0:
                wt = [(SE_, last_copy), ld_wg]
                if t >= 1:
                    wt += [(VE_, stA["negmax"][t - 1]), (SE_, stA["pexp"][t - 1])]
            stA["logit"][t] = mm(pb[5][:, 0:NE],
                                 xnf[bi][:, ec * 128:(ec + 1) * 128],
                                 wg_sb[:, ec * NE:(ec + 1) * NE],
                                 start=(ec == 0), stop=(ec == ECH - 1), wait=wt,
                                 inc=(PE_ if ec == ECH - 1 else None))

        pexp, cmb = gt32[:, 0:32], gt32[:, 32:64]
        top8, cum = gt8[:, 0:8], gt8[:, 8:16]
        keep, kept = gt8[:, 16:24], gt8[:, 24:32]
        tbig, t2c = gt8[:, 32:40], gt8[:, 40:48]
        stA["negmax"][t] = ve(lambda e: e.reduce_max(
            vcol(NEGMAX), pb[5][:, 0:NE], axis=AX.X, negate=True),
            wait=[(PE_, stA["logit"][t])], drain=True)
        stA["pexp"][t] = se(lambda e, p_=pexp: e.activation(
            p_, pb[5][:, 0:NE], AF.Exp, bias=vcol(NEGMAX)),
            wait=[(VE_, stA["negmax"][t])], drain=True)
        ve(lambda e, o=top8, i=pexp: e.max(out=o, in_=i),
           wait=[(SE_, stA["pexp"][t])], drain=True)
        ve(lambda e, o=cum, i=top8: e.tensor_tensor_scan(o, i, i, 0.0,
                                                         ALU.add, ALU.bypass),
           drain=True)
        ve(lambda e, i=cum: e.tensor_scalar(vcol(THRV), i[:, 7:8], 0.8, None,
                                            ALU.mult), drain=True)
        ve(lambda e, o=keep: e.memset(o[:, 0:1], 1.0))
        ve(lambda e, o=keep, i=cum: e.tensor_scalar(o[:, 1:8], i[:, 0:7],
                                                    vcol(THRV), None, ALU.is_lt),
           drain=True)
        ve(lambda e, o=kept, a=top8, b=keep: e.tensor_tensor(o, a, b, ALU.mult),
           drain=True)
        ve(lambda e, i=kept: e.tensor_reduce(vcol(DENOM), i, axis=AX.X,
                                             op=ALU.add), drain=True)
        ve(lambda e: e.reciprocal(vcol(RDEN), vcol(DENOM)), drain=True)
        ve(lambda e, o=tbig, i=keep: e.tensor_scalar(o, i, -BIG, BIG,
                                                     ALU.mult, ALU.add))
        ve(lambda e, o=t2c, a=tbig, b=kept: e.tensor_tensor(o, a, b, ALU.add),
           drain=True)
        ve(lambda e, i=t2c: e.tensor_reduce(vcol(BMIN), i, axis=AX.X,
                                            op=ALU.min), drain=True)
        ve(lambda e, o=cmb, i=pexp: e.scalar_tensor_tensor(
            o, i, vcol(BMIN), i, ALU.is_ge, ALU.mult), drain=True)
        ve(lambda e, d=combine[:, t * NE:(t + 1) * NE], i=cmb:
           e.tensor_scalar(d, i, vcol(RDEN), None, ALU.mult), drain=True)

    stB1 = {"wsl": {}, "proj": {}, "xntf": {}, "st": {}, "ld": {}}

    def emit_B1_wload(h):
        for nme, wdd in (("q", wq_d), ("k", wk_d), ("v", wv_d)):
            w = []
            if h >= 2:
                w = [(PE_, stB1["proj"][h - 2])]
            stB1["wsl"][(h, nme)] = dma_gp(
                wsl[nme][h % 2].rearrange("p (c f) -> p c f", c=ECH),
                wdd[:, h * 128:(h + 1) * 128].rearrange("(c p) f -> p c f", p=128),
                wait=w)

    def emit_B1_ld(m):
        n = m % NB1
        xi = m % 3
        w = []
        if m >= 3:
            w = [(PE_, stB1["xntf"][m - 3])]
        w += [stA["xnst"][t] for t in range(4 * n, 4 * n + 4)]
        stB1["ld"][m] = dma_sy(
            xnt[xi].rearrange("p (c f) -> p c f", c=ECH),
            xn_sc[:, :, n * NTILE:(n + 1) * NTILE].rearrange("c p f -> p c f"),
            wait=w, tag="ld")

    def emit_B1_tile(h, n):
        m = h * NB1 + n
        bi = m % 2
        xi = m % 3
        ld_t = stB1["ld"].pop(m)
        sl = slice(n * NTILE, (n + 1) * NTILE)
        ev_mm = {}
        rot_done = {}
        for nme, bk in (("q", 0), ("k", 1), ("v", 0)):
            for ec in range(ECH):
                wt = []
                if ec == 0:
                    wt = [stB1["wsl"][(h, nme)], ld_t] + bank_free[bk]
                ev_mm[nme] = mm(pb[bk], wsl[nme][h % 2][:, ec * 128:(ec + 1) * 128],
                                xnt[xi][:, ec * NTILE:(ec + 1) * NTILE],
                                start=(ec == 0), stop=(ec == ECH - 1),
                                wait=wt, inc=(PE_ if ec == ECH - 1 else None))
            dst = {"q": qrot_t, "k": krot_t, "v": vfm_t}[nme]
            wse = [(PE_, ev_mm[nme])]
            if m >= 2:
                wse += stB1["st"][m - 2]
            s_cp = se(lambda e, d=dst[bi], s=pb[bk]: e.activation(d, s, AF.Copy),
                      wait=wse)
            bank_free[bk] = [(SE_, s_cp)]
            ev_mm[nme + "cp"] = s_cp
            if nme == "v":
                continue
            ps = pb[bk]
            cosA = cs_c[0:32, sl]
            sinA = cs_s[0:32, sl]
            ve(lambda e, o=rt[1][0:32, :], a=ps[64:96, :], b=sinA:
               e.tensor_tensor(o, a, b, ALU.mult),
               wait=[(SE_, s_cp), ld_csc, ld_css], drain=True)
            v_c = ve(lambda e, o=rt[2][0:32, :], a=ps[64:96, :], b=cosA:
                     e.tensor_tensor(o, a, b, ALU.mult))
            bank_free[bk] = [(SE_, s_cp), (VE_, v_c)]
            ve(lambda e, o=rt[0][0:32, :], a=dst[bi][0:32, :], b=cosA:
               e.tensor_tensor(o, a, b, ALU.mult))
            ve(lambda e, o=rt[3][0:32, :], a=dst[bi][0:32, :], b=sinA:
               e.tensor_tensor(o, a, b, ALU.mult))
            ve(lambda e, o=dst[bi][0:32, :], a=rt[0][0:32, :], b=rt[1][0:32, :]:
               e.tensor_tensor(o, a, b, ALU.add), drain=True)
            rot_done[nme] = ve(
                lambda e, o=dst[bi][64:96, :], a=rt[2][0:32, :], b=rt[3][0:32, :]:
                e.tensor_tensor(o, a, b, ALU.subtract))
        stB1["proj"][h] = ev_mm["v"]
        stB1["xntf"][m] = ev_mm["v"]

        ev_ktm = ev_vtm = None
        for cc in range(4):
            wt = [(VE_, rot_done["k"])] + bank_free[1]
            p_tk = tr(pbf(1), krot_t[bi][:, cc * 128:(cc + 1) * 128], wait=wt,
                      inc=PE_)
            wt2 = [(SE_, ev_mm["vcp"])] + bank_free[0]
            p_tv = tr(pbf(0), vfm_t[bi][:, cc * 128:(cc + 1) * 128], wait=wt2,
                      inc=PE_)
            wk_ = [(PE_, p_tk), ld_kd]
            if m >= 2:
                wk_ += stB1["st"][m - 2]
            ev_ktm = ve(lambda e, o=ktm_t[bi][:, cc * 128:(cc + 1) * 128],
                        s=pbf(1), hh=h:
                        e.tensor_scalar(o, s, kd_sb[:, hh:hh + 1], None, ALU.mult),
                        wait=wk_)
            bank_free[1] = [(VE_, ev_ktm)]
            wv_ = [(PE_, p_tv)]
            if m >= 2:
                wv_ += stB1["st"][m - 2]
            ev_vtm = ve(lambda e, o=vtm_t[bi][:, cc * 128:(cc + 1) * 128],
                        s=pbf(0): e.tensor_copy(o, s), wait=wv_)
            bank_free[0] = [(VE_, ev_vtm)]

        sts = []
        sts.append(dma_sy(qk_sc[0, h, :, sl], qrot_t[bi],
                          wait=[(VE_, rot_done["q"])]))
        sts.append(dma_sy(qk_sc[1, h, :, sl], krot_t[bi],
                          wait=[(VE_, rot_done["k"])]))
        sts.append(dma_sy(tm_sc[0, h, n * NTILE:(n + 1) * NTILE, :]
                          .rearrange("(b s) d -> s b d", b=4),
                          ktm_t[bi].rearrange("p (b d) -> p b d", b=4),
                          wait=[(VE_, ev_ktm)]))
        sts.append(dma_sy(tm_sc[1, h, n * NTILE:(n + 1) * NTILE, :]
                          .rearrange("(b s) d -> s b d", b=4),
                          vtm_t[bi].rearrange("p (b d) -> p b d", b=4),
                          wait=[(VE_, ev_vtm)]))
        stB1["st"][m] = sts

    b2_prev_kvbf = [None, None]
    b2_sbi_guard = {}
    b2_attn_guard = {}

    class B2Pair:
        def __init__(self, p):
            self.p = p
            self.c = 0
            self.attn_st = {}
            self.slice_ld = {}
            self.attnev = [None, None]
            self.load_slice(0)
            self.load_slice(1)
            self.load_slice(2)
            wkv = [(PE_, sch.cur(PE_))]
            wkv += [x for x in b2_prev_kvbf if x is not None]
            ve(lambda e: e.memset(kv_sb[0], 0.0), wait=wkv, drain=True)
            ve(lambda e: e.memset(kv_sb[1], 0.0))
            ve(lambda e: e.memset(kv_bf[0], 0.0))
            v3 = ve(lambda e: e.memset(kv_bf[1], 0.0))
            self.kvbf = [(VE_, v3), (VE_, v3)]

        def load_slice(self, s):
            if s >= NSL:
                return
            sbi = s % 3
            evs = []
            w = []
            if sbi in b2_sbi_guard:
                w = [(PE_, b2_sbi_guard[sbi])]
            for hp in range(2):
                h = self.p * 2 + hp
                t0 = (s * SLICE) // NTILE
                for tt in range(t0, t0 + SLICE // NTILE):
                    w += stB1["st"][h * NB1 + tt]
                s0 = s * SLICE
                evs.append(dma_sy(b2buf[hp][0][sbi],
                                  qk_sc[0, h, :, s0:s0 + SLICE], wait=w, tag="ld"))
                evs.append(dma_sy(b2buf[hp][1][sbi],
                                  qk_sc[1, h, :, s0:s0 + SLICE], wait=w, tag="ld"))
                evs.append(dma_sy(
                    b2buf[hp][2][sbi].rearrange("p (b d) -> p b d", b=CPS),
                    tm_sc[0, h, s0:s0 + SLICE, :]
                    .rearrange("(b s) d -> s b d", b=CPS), wait=w, tag="ld"))
                evs.append(dma_sy(
                    b2buf[hp][3][sbi].rearrange("p (b d) -> p b d", b=CPS),
                    tm_sc[1, h, s0:s0 + SLICE, :]
                    .rearrange("(b s) d -> s b d", b=CPS), wait=w, tag="ld"))
                w = []
            self.slice_ld[s] = evs

        def step(self):
            c = self.c
            if c >= NCH:
                return False
            sbi = (c // CPS) % 3
            cc = c % CPS
            if cc == 0 and c > 0:
                self.load_slice(c // CPS + 2)
            last_kv = 0
            for hp in range(2):
                h = self.p * 2 + hp
                ba, bb, bc = 2 + 3 * hp, 3 + 3 * hp, 4 + 3 * hp
                qs = b2buf[hp][0][sbi][:, cc * 128:(cc + 1) * 128]
                ks = b2buf[hp][1][sbi][:, cc * 128:(cc + 1) * 128]
                kts = b2buf[hp][2][sbi][:, cc * 128:(cc + 1) * 128]
                vts = b2buf[hp][3][sbi][:, cc * 128:(cc + 1) * 128]
                wt = list(self.slice_ld[c // CPS]) + bank_free[ba]
                if c == 0:
                    wt.append(ld_dmt)
                p_sc = mm(pb[ba][:, 0:128], ks, qs, start=True, stop=True,
                          wait=wt, inc=PE_)
                v_mask = ve(lambda e, o=sc_bf[hp], s=pb[ba][:, 0:128], hh=h:
                            e.tensor_tensor(o, s,
                                            dmt_sb[:, hh * 128:(hh + 1) * 128],
                                            ALU.mult), wait=[(PE_, p_sc)])
                p_ia = mm(pb[ba][:, 0:128], sc_bf[hp], vts, start=True, stop=True,
                          wait=[(VE_, v_mask)], inc=PE_)
                wt = [self.kvbf[hp]] + bank_free[bb]
                p_int = mm(pb[bb][:, 0:128], qs, kv_bf[hp], start=True, stop=True,
                           wait=wt, inc=PE_)
                ws = [(PE_, p_int), ld_qd]
                if self.attnev[hp] is not None:
                    ws.append((VE_, self.attnev[hp]))
                s_tint = se(lambda e, o=tmp_int[hp], hh=h, s=pb[bb][:, 0:128]:
                            e.activation(o, s, AF.Copy,
                                         scale=qd_sb[:, hh:hh + 1]), wait=ws)
                bank_free[bb] = [(SE_, s_tint)]
                p_kv = mm(pb[bc][:, 0:128], kts, vts, start=True, stop=True,
                          wait=list(bank_free[bc]), inc=PE_)
                last_kv = p_kv
                v_kvu = ve(lambda e, rc=rC[h], hh=hp, s=pb[bc][:, 0:128]:
                           e.scalar_tensor_tensor(kv_sb[hh], kv_sb[hh], rc, s,
                                                  ALU.mult, ALU.add),
                           wait=[(PE_, p_kv), self.kvbf[hp]], drain=True)
                bank_free[bc] = [(VE_, v_kvu)]
                s_kvbf = se(lambda e, hh=hp: e.activation(kv_bf[hh], kv_sb[hh],
                                                          AF.Copy),
                            wait=[(VE_, v_kvu), (PE_, p_int)], drain=True)
                self.kvbf[hp] = (SE_, s_kvbf)

                grp = c // 4
                abuf = attn_ch[hp][grp % 2]
                wt = [(SE_, s_tint), (PE_, p_ia)]
                if c % 4 == 0 and (hp, grp % 2) in b2_attn_guard:
                    wt.append(b2_attn_guard[(hp, grp % 2)])
                v_att = ve(lambda e, o=abuf[:, (c % 4) * 128:(c % 4 + 1) * 128],
                           t_=tmp_int[hp], s=pb[ba][:, 0:128]:
                           e.tensor_tensor(o, t_, s, ALU.add), wait=wt)
                bank_free[ba] = [(VE_, v_att)]
                self.attnev[hp] = v_att
                if c % 4 == 3:
                    b2_attn_guard[(hp, grp % 2)] = dma_sy(
                        attn_sc[h, (c - 3) * 128:(c + 1) * 128, :]
                        .rearrange("(b s) d -> s b d", b=4),
                        abuf.rearrange("p (b d) -> p b d", b=4),
                        wait=[(VE_, v_att)])
            if cc == CPS - 1:
                b2_sbi_guard[sbi] = last_kv
            self.c += 1
            if self.c == NCH:
                b2_prev_kvbf[0] = self.kvbf[0]
                b2_prev_kvbf[1] = self.kvbf[1]
            return True

    b2_pending = []
    cur_pair = [None]

    def b2_steps(k):
        for _ in range(k):
            if cur_pair[0] is None or cur_pair[0].c >= NCH:
                if b2_pending:
                    cur_pair[0] = B2Pair(b2_pending.pop(0))
                else:
                    return
            cur_pair[0].step()

    emit_B1_wload(0)
    b1_next = 0
    emit_A_ld(0)
    for t in range(NT):
        if t + 1 < NT:
            emit_A_ld(t + 1)
        emit_A_tile(t)
        if t % 4 == 3:
            emit_B1_ld(b1_next)
            emit_B1_tile(0, b1_next)
            b1_next += 1
    assert b1_next == NB1
    ld_next = NB1
    for h in range(1, H):
        emit_B1_wload(h)
        for n in range(NB1):
            m = h * NB1 + n
            while ld_next <= min(m + 2, H * NB1 - 1):
                emit_B1_ld(ld_next)
                ld_next += 1
            emit_B1_tile(h, n)
            b2_steps(2)
        if h % 2 == 1:
            b2_pending.append(h // 2)
    wo_hoist_w = [(PE_, stB1["proj"][H - 1])]
    wo_hoist_w += [(nm, sch.cur(nm)) for nm in ring_ld.names if sch.cur(nm) > 0]
    wo_hoist_w += [(nm, sch.cur(nm)) for nm in ring_gw.names if sch.cur(nm) > 0]
    ev_wo = []
    for c0 in range(0, ECH, 4):
        ev_wo.append(dma_gp(
            wo_sb.rearrange("p (c f) -> p c f", c=ECH)[:, c0:c0 + 4, :],
            wo_d[c0 * 128:(c0 + 4) * 128, :].rearrange("(c p) f -> p c f", p=128),
            wait=wo_hoist_w))
        wo_hoist_w = []

    while b2_pending or (cur_pair[0] is not None and cur_pair[0].c < NCH):
        b2_steps(8)

    sch.barrier()
    bank_free = [[] for _ in range(8)]

    v_wos = None
    for ec in range(ECH):
        v_wos = ve(lambda e, d=wo_sb[:, ec * E:(ec + 1) * E], ecc=ec:
                   e.tensor_scalar(d, d, rmsw_sb[:, ecc:ecc + 1], None, ALU.mult),
                   wait=ev_wo + [ld_rmsw])

    stC = {"ain": {}, "xld": {}, "sumsq": {}, "attnn": {}, "ln": {}, "tp": {},
           "womm": {}, "xat": {}, "xatst": {}}

    def emit_C_ld(t):
        bi = t % 2
        w = []
        if t >= 2:
            w = [(VE_, stC["sumsq"][t - 2]), (SE_, stC["attnn"][t - 2])]
        stC["ain"][t] = dma_sy(
            attn_in[bi].rearrange("p (h d) -> p h d", h=H),
            attn_sc[:, t * 128:(t + 1) * 128, :].rearrange("h s d -> s h d"),
            wait=w, tag="ld")
        w = []
        if t >= 2:
            w = [stC["xatst"][t - 2]]
        stC["xld"][t] = dma_sy(xbuf[bi], x_d[t * 128:(t + 1) * 128, :], wait=w,
                               tag="ld")

    def emit_C_tile(t):
        bi = t % 2
        vecs = vecs2[bi]

        def vcol(i):
            return vecs[:, i:i + 1]

        ld_xc = stC["xld"].pop(t)
        wv = [stC["ain"][t]]
        if t >= 2:
            wv.append((SE_, stC["attnn"][t - 2]))
        stC["sumsq"][t] = ve(lambda e, a=attn_in[bi]: e.scalar_tensor_tensor(
            trash, a, 1.0, a, ALU.mult, ALU.mult, accum_out=vcol(SUMSQ)),
            wait=wv, drain=True)
        v_var = ve(lambda e: e.tensor_scalar(vcol(VARV), vcol(SUMSQ), INV_E, EPS,
                                             ALU.mult, ALU.add), drain=True)
        stC["ln"][t] = se(lambda e: e.activation(vcol(LNV), vcol(VARV), AF.Ln),
                          wait=[(VE_, v_var)])
        s_rstd = se(lambda e: e.activation(vcol(RSTD), vcol(LNV), AF.Exp,
                                           scale=vcol(NEGHALF)), drain=True)
        w = [(SE_, s_rstd)]
        if t >= 1:
            w.append((PE_, stC["tp"][t - 1]))
        stC["attnn"][t] = se(lambda e, a=attn_in[bi]: e.activation(
            attnn_tm, a, AF.Copy, scale=vcol(RSTD)), wait=w, drain=True)

        last_copy = None
        for ec in range(ECH):
            bk = 4 + ec % 2
            wt = [(SE_, stC["attnn"][t])] + bank_free[bk]
            stC["tp"][t] = tr(pbf(bk), attnn_tm[:, ec * 128:(ec + 1) * 128],
                              wait=wt, inc=PE_)
            wv = [(PE_, stC["tp"][t])]
            if ec == 0 and t >= 2:
                wv.append((PE_, stC["womm"][t - 2]))
            last_copy = ve(lambda e, d=attnnf[bi][:, ec * 128:(ec + 1) * 128],
                           s=pbf(bk): e.tensor_copy(d, s), wait=wv)
            bank_free[bk] = [(VE_, last_copy)]

        for ot in range(4):
            for ec in range(ECH):
                wt = []
                if ec == 0:
                    wt = [(VE_, last_copy), (VE_, v_wos)] + bank_free[ot]
                stC["womm"][t] = mm(pb[ot], attnnf[bi][:, ec * 128:(ec + 1) * 128],
                                    wo_sb[:, ec * E + ot * 512:
                                          ec * E + ot * 512 + 512],
                                    start=(ec == 0), stop=(ec == ECH - 1),
                                    wait=wt, inc=(PE_ if ec == ECH - 1 else None))
            stC["xat"][t] = ve(lambda e, o=xbuf[bi][:, ot * 512:(ot + 1) * 512],
                               s=pb[ot]: e.tensor_tensor(o, s, o, ALU.add),
                               wait=[(PE_, stC["womm"][t]), ld_xc])
            bank_free[ot] = [(VE_, stC["xat"][t])]
        stC["xatst"][t] = dma_sy(xat_sc[t * 128:(t + 1) * 128, :], xbuf[bi],
                                 wait=[(VE_, stC["xat"][t])])

    emit_C_ld(0)
    for t in range(NT):
        if t + 1 < NT:
            emit_C_ld(t + 1)
        emit_C_tile(t)

    sch.barrier(engines=("pe", "ve", "se", "sy"))
    bank_free = [[] for _ in range(8)]

    ev_wgu = {}
    ev_wd = {}
    pair_mm_done = {}
    wd_mm_done = {}
    ev_down_mm = {}
    ev_acc_ev = {}
    ev_y_st = {}
    ev_xng = {}
    ev_accinit = {}
    down_all = {}
    hm_ev = {}
    tsu = {}
    N_PAIR = NG * NE * ICH
    N_WD = NG * NE * 4

    def emit_wgu(pr, wait=()):
        e2 = (pr // ICH) % NE
        jj2 = pr % ICH
        ev_wgu[pr] = dma_gp(
            wgu_sl[pr % 2].rearrange("p (c f) -> p c f", c=ECH),
            wgu_d[e2, :, jj2 * 256:(jj2 + 1) * 256]
            .rearrange("(c p) f -> p c f", p=128), wait=wait)

    def emit_wd(n2, wait=()):
        e2 = (n2 // 4) % NE
        ot2 = n2 % 4
        ev_wd[n2] = dma_gp(
            wd_sl[n2 % 2].rearrange("p (c f) -> p c f", c=ICH),
            wd_d[e2, :, ot2 * 512:(ot2 + 1) * 512]
            .rearrange("(c p) f -> p c f", p=128), wait=wait)

    emit_wgu(0)
    emit_wgu(1)
    emit_wd(0)
    emit_wd(1)
    last_gu_mm_of_group = 0

    def emit_y_st(g):
        for tl in range(TPG):
            t0 = g * G + tl * 128
            ev_y_st[(g, tl)] = dma_sy(y_d[t0:t0 + 128, :],
                                      acc[:, tl * E:(tl + 1) * E],
                                      wait=[(VE_, ev_acc_ev[(g, tl)])])

    for g in range(NG):
        ev_xng[g] = {}
        for ec_ in range(ECH):
            w = []
            if g >= 1 and ec_ == 0:
                w = [(PE_, last_gu_mm_of_group)]
            ev_xng[g][ec_] = dma_sy(
                xn_g[:, ec_ * G:(ec_ + 1) * G],
                xn_sc[ec_, :, g * G:(g + 1) * G],
                wait=w, tag="ld")
        if g >= 1:
            emit_y_st(g - 1)
        for tl in range(TPG):
            t0 = g * G + tl * 128
            w = []
            if g >= 1:
                w = [ev_y_st[(g - 1, tl)]]
            ev_accinit[(g, tl)] = dma_sy(acc[:, tl * E:(tl + 1) * E],
                                         xat_sc[t0:t0 + 128, :], wait=w, tag="ld")

        for e_ in range(NE):
            eg = g * NE + e_
            hb = eg % 2
            for jj in range(ICH):
                pr = eg * ICH + jj
                for nt in range(NTG):
                    k = jj * NTG + nt
                    gA, gB = 2 * (k % 2), 2 * (k % 2) + 1
                    mmg = mmu = None
                    for half, bk in ((0, gA), (1, gB)):
                        for ec in range(ECH):
                            wt = [ev_xng[g][ec]]
                            if ec == 0:
                                wt += [ev_wgu[pr]] + bank_free[bk]
                            r = mm(pb[bk],
                                   wgu_sl[pr % 2][:, ec * 256 + half * 128:
                                                  ec * 256 + half * 128 + 128],
                                   xn_g[:, ec * G + nt * 512:
                                        ec * G + nt * 512 + 512],
                                   start=(ec == 0), stop=(ec == ECH - 1),
                                   wait=wt,
                                   inc=(PE_ if ec == ECH - 1 else None))
                            if half == 0:
                                mmg = r
                            else:
                                mmu = r
                    wsi = [(PE_, mmg)]
                    if k % 2 in tsu:
                        wsi.append((VE_, tsu[k % 2]))
                    s_si = se(lambda e, o=tmp_silu[k % 2], s=pb[gA]:
                              e.activation(o, s, AF.Silu), wait=wsi)
                    wv = [(SE_, s_si), (PE_, mmu)]
                    if jj == 0 and nt == 0 and eg >= 2:
                        wv.append((PE_, down_all[eg - 2]))
                    v_h = ve(lambda e, o=h_sb[hb][:, jj * G + nt * 512:
                                                  jj * G + nt * 512 + 512],
                             si=tmp_silu[k % 2], u=pb[gB]:
                             e.tensor_tensor(o, si, u, ALU.mult), wait=wv)
                    hm_ev[(eg, jj, nt)] = v_h
                    tsu[k % 2] = v_h
                    bank_free[gA] = [(SE_, s_si)]
                    bank_free[gB] = [(VE_, v_h)]
                pair_mm_done[pr] = mmu
                if pr + 2 < N_PAIR:
                    emit_wgu(pr + 2, wait=[(PE_, pair_mm_done[pr])])

            first_evac = True
            v_ev = 0
            for ot in range(4):
                n2 = eg * 4 + ot
                for tl in range(TPG):
                    bk = 4 + (ot * TPG + tl) % 4
                    grp = ot * TPG + tl
                    ntt = (tl * 128) // 512
                    for kc in range(ICH):
                        wt = [(VE_, hm_ev[(eg, kc, ntt)])]
                        if kc == 0:
                            wt += [ev_wd[n2]] + bank_free[bk]
                        ev_down_mm[(eg, grp)] = mm(
                            pb[bk],
                            h_sb[hb][:, kc * G + tl * 128:kc * G + tl * 128 + 128],
                            wd_sl[n2 % 2][:, kc * 512:(kc + 1) * 512],
                            start=(kc == 0), stop=(kc == ICH - 1), wait=wt,
                            inc=(PE_ if kc == ICH - 1 else None))
                    ce = combine[:, (g * TPG + tl) * NE + e_:
                                 (g * TPG + tl) * NE + e_ + 1]
                    wv = [(PE_, ev_down_mm[(eg, grp)]), ev_accinit[(g, tl)]]
                    v_ev = ve(lambda e, p=pb[bk], c=ce,
                              a=acc[:, tl * E + ot * 512:tl * E + ot * 512 + 512]:
                              e.scalar_tensor_tensor(a, p, c, a, ALU.mult,
                                                     ALU.add),
                              wait=wv, drain=first_evac)
                    first_evac = False
                    bank_free[bk] = [(VE_, v_ev)]
                    ev_acc_ev[(g, tl)] = v_ev
                wd_mm_done[n2] = ev_down_mm[(eg, ot * TPG + TPG - 1)]
                if n2 + 2 < N_WD:
                    emit_wd(n2 + 2, wait=[(PE_, wd_mm_done[n2])])
            down_all[eg] = ev_down_mm[(eg, TPG * 4 - 1)]
        last_gu_mm_of_group = pair_mm_done[(g * NE + NE - 1) * ICH + ICH - 1]
    emit_y_st(NG - 1)

    sem_names = [PE_, VE_, SE_] + ring_ld.names + ring_st.names + ring_gw.names
    sems = {name: nc.alloc_semaphore(name) for name in sem_names}

    with nc.Block() as block:
        def runner(engname):
            def run(eng):
                waited = {}
                for op_eng, fn, waits, incs, drain in sch.ops:
                    if op_eng != engname:
                        continue
                    if drain:
                        eng.drain()
                    for s_, v in waits:
                        if v > 0 and v > waited.get(s_, 0):
                            eng.wait_ge(sems[s_], v)
                            waited[s_] = v
                    inst = fn(eng)
                    for s_, a in incs:
                        inst.then_inc(sems[s_], a)
                if engname == "sy":
                    for nm in ring_st.names:
                        if sch.cur(nm) > 0:
                            eng.wait_ge(sems[nm], sch.cur(nm))
            return run
        block.sync(runner("sy"))
        block.gpsimd(runner("gp"))
        block.tensor(runner("pe"))
        block.vector(runner("ve"))
        block.scalar(runner("se"))

    return nc


def _prep_inputs(L0_attn_x, L0_attn_Wq, L0_attn_Wk, L0_attn_Wv, L0_attn_Wo,
                 L0_attn_rms_w, L0_ffn_gate_weight, L0_ffn_gate_up_proj,
                 L0_ffn_down_proj):
    c32 = lambda a: np.ascontiguousarray(np.asarray(a, dtype=np.float32))
    x = c32(L0_attn_x)
    consts, _ = _consts()
    wgu = np.asarray(L0_ffn_gate_up_proj, np.float32).transpose(0, 2, 1)
    idx = np.concatenate([np.r_[jj * 128:(jj + 1) * 128,
                                (jj + ICH) * 128:(jj + ICH + 1) * 128]
                          for jj in range(ICH)])
    wgu = np.ascontiguousarray(wgu[:, :, idx])
    wd = np.ascontiguousarray(np.asarray(L0_ffn_down_proj, np.float32)
                              .transpose(0, 2, 1))
    base = {
        "wq": c32(L0_attn_Wq), "wk": c32(L0_attn_Wk), "wv": c32(L0_attn_Wv),
        "wo": c32(L0_attn_Wo),
        "wg": np.ascontiguousarray(np.asarray(L0_ffn_gate_weight, np.float32).T),
        "wgu": wgu, "wd": wd,
        "rmsw": np.ascontiguousarray(c32(L0_attn_rms_w).reshape(ECH, 128).T),
        "csc": consts["csc"], "css": consts["css"], "dmt": consts["dmt"],
        "qdc": consts["qd"], "kdc": consts["kd"], "idm": consts["ident"],
    }
    return x, base


def kernel(L0_attn_x, L0_attn_Wq, L0_attn_Wk, L0_attn_Wv, L0_attn_Wo,
           L0_attn_rms_w, L0_ffn_gate_weight, L0_ffn_gate_up_proj,
           L0_ffn_down_proj):
    x, base = _prep_inputs(L0_attn_x, L0_attn_Wq, L0_attn_Wk, L0_attn_Wv,
                           L0_attn_Wo, L0_attn_rms_w, L0_ffn_gate_weight,
                           L0_ffn_gate_up_proj, L0_ffn_down_proj)
    nc = build_nc()
    in_maps = [dict(base, x=np.ascontiguousarray(x[c])) for c in range(NCORES)]
    res = run_bass_kernel_spmd(nc, in_maps, core_ids=list(range(NCORES)))
    return np.stack([np.asarray(r["y"], np.float32) for r in res.results], axis=0)
